# version 01 (final)
# baseline (speedup 1.0000x reference)
import contextlib
import numpy as np
import concourse.bass as bass
import concourse.mybir as mybir
from concourse.bass_utils import run_bass_kernel_spmd
from concourse import library_config as LC
from concourse.library_overlay import lower_extended_insts
from concourse.bass_primitives import MemorySpace

F32 = mybir.dt.float32
F16 = mybir.dt.float16
I16 = mybir.dt.int16
U32 = mybir.dt.uint32
AF = mybir.ActivationFunctionType
ALU = mybir.AluOpType
AX = mybir.AxisListType

S, D, C, H, DH = 4096, 1024, 2048, 16, 64
NT = S // 128
DC = D // 128
TC = S // 128
NB = S // 512
FTC = C // 128
FQB = C // 512
ROWB = D + 128
MARGIN = 0.25
NROUND = 10

DEBUG = ()


class Sems:
    def __init__(self, nc):
        self.nc = nc
        self.v = {}
        self.h = {}

    def sem(self, name):
        if name not in self.h:
            self.h[name] = self.nc.alloc_semaphore(name)
            self.v[name] = 0
        return self.h[name]

    def inc(self, instr, name, amt=1):
        s = self.sem(name)
        self.v[name] += amt
        instr.then_inc(s, amt)
        return self.v[name]

    def val(self, name):
        self.sem(name)
        return self.v[name]

    def wait(self, eng, name, val):
        if val <= 0:
            return
        eng.wait_ge(self.sem(name), val)

    def fence(self, ins):
        return self.inc(ins, "vf")

    def fwait(self, eng, val):
        self.wait(eng, "vf", val)




class SbufPlan:
    def __init__(self, nc, base, cap):
        self.nc = nc
        self.base = (base + 63) & ~63
        self.cap = cap
        self.items = []

    def alloc(self, name, shape, dtype, birth, death, top=False):
        per = int(np.prod(shape[1:])) * mybir.dt.size(dtype)
        size = (per + 63) & ~63
        live = sorted((o, s) for (o, s, b, d, n) in self.items
                      if not (death <= b or d <= birth))
        holes = []
        cur = self.base
        for (o, s) in live:
            if o > cur:
                holes.append((cur, o))
            cur = max(cur, o + s)
        if cur < self.cap:
            holes.append((cur, self.cap))
        off = None
        cands = [h for h in holes if h[1] - ((h[0] + 63) & ~63) >= size]
        assert cands, f"sbuf plan overflow: {name} size {size} holes {holes}"
        if top:
            h = cands[-1]
            off = (h[1] - size) & ~63
            assert off >= h[0]
        else:
            h = cands[0]
            off = (h[0] + 63) & ~63
        self.items.append((off, size, birth, death, name))
        return self.nc.alloc_sbuf_tensor_at(name, shape, dtype, offset=off)


def build_consts():
    c32 = np.zeros((128, 448), np.float32)
    c32[:, 3] = 1.0
    c32[0, 4:132] = 1.0
    tr = np.zeros((32, 32), np.float32)
    for p in range(32):
        tr[p, p + 1:] = 1.0
    c32[0:32, 132:164] = tr
    c32[:, 164:292] = np.eye(128, dtype=np.float32)
    c32[0, 292:308] = np.arange(1, 17, dtype=np.float32)
    c32[0:32, 308:436] = (np.arange(32)[:, None] * 128
                          + np.arange(128)[None, :]).astype(np.float32)
    c16 = np.eye(128, dtype=np.float16)
    ia = np.zeros((32, 256), np.int16)
    for i in range(S):
        ia[i % 16, i // 16] = i
    ia[16:32] = ia[0:16]
    return c32, c16, ia


def build_program(debug=DEBUG, limit=99):
    nc = bass.Bass("TRN2", target_bir_lowering=False)
    sm = Sems(nc)

    DT = nc.dram_tensor
    x = DT("x", [S, D], F32, kind="ExternalInput").ap()
    wq = DT("wq", [129, D], F32, kind="ExternalInput").ap()
    bq = DT("bq", [129, 1], F32, kind="ExternalInput").ap()
    wk = DT("wk", [129, D], F32, kind="ExternalInput").ap()
    bk = DT("bk", [129, 1], F32, kind="ExternalInput").ap()
    wv = DT("wv", [D, D], F32, kind="ExternalInput").ap()
    bvT = DT("bv", [1, D], F32, kind="ExternalInput").ap()
    k0T = DT("k0", [129, 1], F32, kind="ExternalInput").ap()
    v0T = DT("v0", [1, D], F32, kind="ExternalInput").ap()
    wgT = DT("wg", [1, D], F32, kind="ExternalInput").ap()
    Wqf = DT("Wqf", [D, D], F32, kind="ExternalInput").ap()
    Wkf = DT("Wkf", [D, D], F32, kind="ExternalInput").ap()
    Wvf = DT("Wvf", [D, D], F32, kind="ExternalInput").ap()
    Wof = DT("Wof", [D, D], F32, kind="ExternalInput").ap()
    cst32 = DT("cst32", [128, 448], F32, kind="ExternalInput").ap()
    cst16 = DT("cst16", [128, 128], F16, kind="ExternalInput").ap()
    csti = DT("csti", [32, 256], I16, kind="ExternalInput").ap()
    y = DT("y", [S, D], F32, kind="ExternalOutput").ap()
    sc_hi = DT("sc_hi", [S, ROWB], F16, kind="Internal").ap()
    sc_lo = DT("sc_lo", [S, D], F16, kind="Internal").ap()
    sc_z = DT("sc_z", [1, 16, 128], F16, kind="Internal").ap()

    dbg = {}
    def DB(name, shape, dt_=F32):
        if name not in dbg:
            dbg[name] = DT("dbg_" + name, shape, dt_, kind="ExternalOutput").ap()
        return dbg[name]

    def dbg_dump(items):
        with nc.Block() as b:
            @b.sync
            def _(s):
                for name, shape, dt_, src in items:
                    dst = DB(name, shape, dt_)
                    idx = tuple(slice(None) for _ in shape)
                    ins = s.dma_start(dst[idx], src, allow_slow_non_contiguous=True)
                    sm.inc(ins, "dbg", 16)
                sm.wait(s, "dbg", sm.val("dbg"))

    A = nc.alloc_sbuf_tensor
    c32 = A("c32", [128, 448], F32)
    id16 = A("id16", [128, 128], F16)
    idxall = A("idxall", [32, 256], I16)
    ones16 = A("ones16", [128, 1], F16)
    blk2 = A("blk2", [128, 2], F16)
    mcol = A("mcol", [128, NT], F32)
    rstdcol = A("rstdcol", [128, NT], F32)
    zcol = A("zcol", [128, NT], F32)
    bqkc = A("bqkc", [128, 2], F32)
    b128 = A("b128", [2, 1], F32)
    k0c32 = A("k0c32", [128, 1], F32)
    k0s = A("k0s", [2, 1], F32)
    zTt = A("zTt", [32, 128], F32)
    srch = A("srch", [1, 16], F32)
    mids = A("mids", [1, 16], F32)
    midsB = A("midsB", [32, 16], F32)
    cnt = A("cnt", [32, 16], F32)
    junk32 = A("junk32", [32, 128], F32)
    mhi = A("mhi", [32, 128], F32)
    tie = A("tie", [32, 128], F32)
    rnkh = A("rnkh", [32, 128], F32)
    rnkt = A("rnkt", [32, 128], F32)
    loB = A("loB", [32, 2], F32)
    arr = A("arr", [32, 128], F32)
    arr16 = A("arr16", [16, 256], F32)
    sgout = A("sgout", [16, 128], F32)
    nfound = A("nfound", [1, 1], U32)
    idx16 = A("idx16", [32, 128], I16)
    gsel = A("gsel", [128, 16], F32)
    swgc = A("swgc", [128, 1], F32)
    epsc = A("epsc", [128, 1], F32)

    def onerow(n):
        return c32[0:1, 4:4 + n]
    def onecol(p):
        return c32[0:p, 3:4]
    def id32(p, f):
        return c32[0:p, 164:164 + f]
    tril32 = c32[0:32, 132:164]
    j16row = c32[0:1, 292:308]
    iotaT = c32[0:32, 308:436]

    pA = nc.alloc_psum_tensor("pA", [128, 4, 512], F32)
    pT = nc.alloc_psum_tensor("pT", [128, 2, 512], F32)
    pAV = nc.alloc_psum_tensor("pAV", [128, 2, 512], F32)
    pA16 = pA.bitcast(F16)

    plan = SbufPlan(nc, nc.bytes_allocated(MemorySpace.SBUF), 229000)
    PA = plan.alloc
    wq16 = PA("wq16", [128, D], F16, 0, 2)
    wk16 = PA("wk16", [128, D], F16, 0, 2)
    wv16s = PA("wv16s", [128, DC, D], F16, 0, 2)
    w128r = PA("w128r", [2, D], F16, 0, 2)
    bvrow = PA("bvrow", [1, D], F32, 0, 2)
    wgrow = PA("wgrow", [1, D], F32, 0, 2)
    wqTt = PA("wqTt", [128, DC, 128], F16, 1, 5)
    wkTt = PA("wkTt", [128, DC, 128], F16, 1, 5)
    w12T = PA("w12T", [128, DC, 2], F16, 1, 5)
    wvTt = PA("wvTt", [128, DC, D], F16, 1, 7)
    wgbc = PA("wgbc", [128, D], F32, 1, 3)
    vsink = PA("vsink", [1, D], F16, 0, 9, top=True)
    bvbc = PA("bvbc", [128, D], F32, 1, 9, top=True)

    with nc.Block() as b:
        @b.sync
        def _(s):
            for dst, src in [
                (c32[:], cst32[:, :]), (id16[:], cst16[:, :]), (idxall[:], csti[:, :]),
                (bqkc[:, 0:1], bq[0:128, :]), (bqkc[:, 1:2], bk[0:128, :]),
                (b128[0:1, :], bq[128:129, :]), (b128[1:2, :], bk[128:129, :]),
                (k0c32[:], k0T[0:128, :]), (k0s[1:2, :], k0T[128:129, :]),
                (bvrow[:], bvT[:, :]), (wgrow[:], wgT[:, :]),
            ]:
                ins = s.dma_start(dst, src, allow_slow_non_contiguous=True)
                sm.inc(ins, "ld", 16)
            sm.wait(s, "ld", sm.val("ld"))
        @b.gpsimd
        def _(g):
            g.load_library(LC.mlp)
            for dst, src in [
                (wq16[:], wq[0:128, :]), (wk16[:], wk[0:128, :]),
                (w128r[0:1, :], wq[128:129, :]), (w128r[1:2, :], wk[128:129, :]),
                (wv16s[:], wv[:, :].rearrange("(c p) n -> p c n", p=128)),
                (vsink[:], v0T[:, :]),
            ]:
                ins = g.dma_start(dst, src)
                sm.inc(ins, "ldw", 16)
            sm.wait(g, "ldw", sm.val("ldw"))

    tr_jobs = []
    for cc in range(DC):
        tr_jobs.append((wq16[:, cc * 128:(cc + 1) * 128], wqTt[:, cc, :], 128))
        tr_jobs.append((wk16[:, cc * 128:(cc + 1) * 128], wkTt[:, cc, :], 128))
        tr_jobs.append((w128r[0:2, cc * 128:(cc + 1) * 128], w12T[:, cc, 0:2], 2))
    for cc in range(DC):
        for dd in range(DC):
            tr_jobs.append((wv16s[:, cc, dd * 128:(dd + 1) * 128],
                            wvTt[:, dd, cc * 128:(cc + 1) * 128], 128))

    with nc.Block() as b:
        @b.tensor
        def _(t):
            for k, (src, dst, pp) in enumerate(tr_jobs):
                if k >= 2:
                    sm.wait(t, "b0v", k - 1)
                ins = t.matmul(pA16[0:dst.shape[0], k % 2, 0:dst.shape[-1]],
                               src, id16[0:pp, 0:pp],
                               start=True, stop=True, is_transpose=True)
                sm.inc(ins, "b0t")
            nj = len(tr_jobs)
            for half in range(2):
                sm.wait(t, "b0v", nj + 2 * half)
                ins = t.matmul(pA[:, 2, :], onerow(128),
                               wgrow[0:1, half * 512:(half + 1) * 512],
                               start=True, stop=True)
                sm.inc(ins, "b0t")
                ins = t.matmul(pA[:, 3, :], onerow(128),
                               bvrow[0:1, half * 512:(half + 1) * 512],
                               start=True, stop=True)
                sm.inc(ins, "b0t")
        @b.vector
        def _(v):
            for k, (src, dst, pp) in enumerate(tr_jobs):
                sm.wait(v, "b0t", k + 1)
                ins = v.tensor_copy(dst, pA16[0:dst.shape[0], k % 2, 0:dst.shape[-1]])
                sm.inc(ins, "b0v")
            nj = len(tr_jobs)
            for half in range(2):
                sm.wait(v, "b0t", nj + 2 * (half + 1))
                v.tensor_copy(wgbc[:, half * 512:(half + 1) * 512], pA[:, 2, :])
                ins = v.tensor_copy(bvbc[:, half * 512:(half + 1) * 512], pA[:, 3, :])
                sm.inc(ins, "b0v", 2)
            v.memset(ones16[:], 1.0)
            v.memset(epsc[:], 1e-5)
            v.tensor_reduce(swgc[:], wgbc[:], axis=AX.X, op=ALU.add)
            v.memset(blk2[:, :], 0.0)
            v.memset(blk2[0:64, 0:1], 1.0)
            v.memset(blk2[64:128, 1:2], 1.0)

    if limit <= 0:
        lower_extended_insts(nc)
        return nc, dbg
    xt = PA("xt", [128, 2, D], F32, 2, 3)
    t16 = PA("t16", [128, 2, ROWB], F16, 2, 3)
    tlo = PA("tlo", [128, 2, D], F16, 2, 3)
    xn32 = PA("xn32", [128, D], F32, 2, 3)
    sq2 = PA("sq2", [128, D], F32, 2, 3)
    junkD = PA("junkD", [128, D], F32, 2, 3)
    lnst = PA("lnst", [128, 8], F32, 2, 3)

    with nc.Block() as b:
        @b.sync
        def _(s):
            for i in range(NT):
                if i >= 2:
                    sm.wait(s, "lnvx", i - 1)
                ins = s.dma_start(xt[:, i % 2, :], x[i * 128:(i + 1) * 128, :])
                sm.inc(ins, "lnx", 16)
        @b.scalar
        def _(s):
            for i in range(NT):
                sm.wait(s, "lnx", (i + 1) * 16)
                ins = s.activation(sq2[:], xt[:, i % 2, :], AF.Square,
                                   scale=1.0 / 32.0, accum_out=lnst[:, 1:2])
                sm.inc(ins, "lns")
                sm.wait(s, "lnv1", i + 1)
                ins = s.activation(lnst[:, 6:7], lnst[:, 4:5], AF.Sqrt,
                                   bias=epsc[:, 0:1])
                sm.inc(ins, "lns2")
        @b.vector
        def _(v):
            for i in range(NT):
                sm.wait(v, "lnx", (i + 1) * 16)
                xti = xt[:, i % 2, :]
                v.tensor_reduce(lnst[:, 0:1], xti, axis=AX.X, op=ALU.add)
                v.tensor_tensor(junkD[:], xti, wgbc[:], op=ALU.mult)
                sm.wait(v, "lns", i + 1)
                v.scalar_tensor_tensor(lnst[:, 3:4], lnst[:, 0:1],
                                       1.0 / (D * D), lnst[:, 0:1],
                                       op0=ALU.mult, op1=ALU.mult)
                v.tensor_scalar(mcol[:, i:i + 1], lnst[:, 0:1], 1.0 / D, None,
                                op0=ALU.mult)
                v.tensor_reduce(lnst[:, 7:8], junkD[:], axis=AX.X, op=ALU.add)
                v.tensor_scalar(lnst[:, 5:6], mcol[:, i:i + 1], swgc[:, 0:1],
                                None, op0=ALU.mult)
                ins = v.tensor_tensor(lnst[:, 4:5], lnst[:, 1:2], lnst[:, 3:4],
                                      op=ALU.subtract)
                sm.inc(ins, "lnv1")
                sm.wait(v, "lns2", i + 1)
                ins = v.reciprocal(rstdcol[:, i:i + 1], lnst[:, 6:7])
                fr = sm.fence(ins)
                if i >= 2:
                    sm.wait(v, "lnw", (i - 1) * 32)
                v.memset(t16[:, i % 2, D + 1:ROWB], 0.0)
                sm.fwait(v, fr)
                ins = v.tensor_scalar(xn32[:], xti, mcol[:, i:i + 1],
                                      rstdcol[:, i:i + 1],
                                      op0=ALU.subtract, op1=ALU.mult)
                sm.inc(ins, "lnvx")
                v.tensor_scalar(t16[:, i % 2, D:D + 1], lnst[:, 7:8],
                                lnst[:, 5:6], rstdcol[:, i:i + 1],
                                op0=ALU.subtract, op1=ALU.mult)
                v.tensor_scalar(zcol[:, i:i + 1], lnst[:, 7:8],
                                lnst[:, 5:6], rstdcol[:, i:i + 1],
                                op0=ALU.subtract, op1=ALU.mult)
                sm.wait(v, "lnvx", i + 1)
                ins = v.tensor_copy(t16[:, i % 2, 0:D], xn32[:])
                fth = sm.fence(ins)
                sm.fwait(v, fth)
                ins = v.scalar_tensor_tensor(tlo[:, i % 2, :], xn32[:], 1.0,
                                             t16[:, i % 2, 0:D],
                                             op0=ALU.mult, op1=ALU.subtract)
                sm.inc(ins, "lnv")
        @b.gpsimd
        def _(g):
            for i in range(NT):
                sm.wait(g, "lnv", i + 1)
                ins = g.dma_start(sc_hi[i * 128:(i + 1) * 128, :], t16[:, i % 2, :])
                sm.inc(ins, "lnw", 16)
                ins = g.dma_start(sc_lo[i * 128:(i + 1) * 128, :], tlo[:, i % 2, :])
                sm.inc(ins, "lnw", 16)
            sm.wait(g, "lnw", NT * 32)


    if "ln" in debug:
        dbg_dump([("mc", [128, NT], F32, mcol[:]),
                  ("rc", [128, NT], F32, rstdcol[:]),
                  ("zc", [128, NT], F32, zcol[:]),
                  ("xt", [128, 2, D], F32, xt[:]),
                  ("swgc", [128, 1], F32, swgc[:]),
                  ("lnst", [128, 8], F32, lnst[:])])

    if limit <= 1:
        lower_extended_insts(nc)
        return nc, dbg
    xnT = PA("xnT", [128, 4, DC, 512], F16, 3, 5)
    qph = PA("qph", [128, S], F32, 3, 6)
    kph = PA("kph", [128, S + 128], F32, 3, 6)
    qk128h = PA("qk128h", [2, S], F32, 3, 6)


    def emit_proj(b, vecfn, half):
        base_t = sm.val("pj_t")
        base_v = sm.val("pj_v")
        @b.tensor
        def _(t):
            for sc in range(4):
                if sc >= 2:
                    sm.wait(t, "pj_v", base_v + sc - 1)
                for cc in range(DC):
                    t.matmul(pA[:, sc % 2, :], wqTt[:, cc, :], xnT[:, sc, cc, :],
                             start=(cc == 0), stop=(cc == DC - 1))
                for cc in range(DC):
                    t.matmul(pA[:, 2 + sc % 2, :], wkTt[:, cc, :], xnT[:, sc, cc, :],
                             start=(cc == 0), stop=(cc == DC - 1))
                ins = None
                for cc in range(DC):
                    ins = t.matmul(pT[0:2, sc % 2, :], w12T[:, cc, :],
                                   xnT[:, sc, cc, :],
                                   start=(cc == 0), stop=(cc == DC - 1))
                sm.inc(ins, "pj_t")
        @b.vector
        def _(v):
            for sc in range(4):
                sm.wait(v, "pj_t", base_t + sc + 1)
                ins = vecfn(v, half * 4 + sc)
                sm.inc(ins, "pj_v")

    def hi_evac(v, sc):
        sl = slice(sc * 512, sc * 512 + 512)
        v.tensor_copy(qph[:, sl], pA[:, sc % 2, :])
        v.tensor_copy(kph[:, sl], pA[:, 2 + sc % 2, :])
        return v.tensor_copy(qk128h[:, sl], pT[0:2, sc % 2, :])

    def lo_evac(v, sc):
        sl = slice(sc * 512, sc * 512 + 512)
        v.scalar_tensor_tensor(qph[:, sl], pA[:, sc % 2, :], bqkc[:, 0:1],
                               qph[:, sl], op0=ALU.add, op1=ALU.add)
        v.scalar_tensor_tensor(kph[:, sl], pA[:, 2 + sc % 2, :], bqkc[:, 1:2],
                               kph[:, sl], op0=ALU.add, op1=ALU.add)
        return v.scalar_tensor_tensor(qk128h[:, sl], pT[0:2, sc % 2, :], b128[:],
                                      qk128h[:, sl], op0=ALU.add, op1=ALU.add)

    for half in range(2):
        with nc.Block() as b:
            @b.gpsimd
            def _(g):
                for gch in range(4):
                    ins = g.dma_gather(xnT[:, gch, :, :], sc_hi[:, 0:D],
                                       idxall[:, (half * 4 + gch) * 32:
                                              (half * 4 + gch + 1) * 32],
                                       512, 512, D, elem_step=ROWB, transpose=True)
                    sm.inc(ins, "gx", 16)
                sm.wait(g, "gx", sm.val("gx"))
        if limit <= 1.1 and half == 0:
            lower_extended_insts(nc)
            return nc, dbg
        with nc.Block() as b:
            emit_proj(b, hi_evac, half)
        if limit <= 1.2 and half == 0:
            lower_extended_insts(nc)
            return nc, dbg
    if limit <= 1.3:
        lower_extended_insts(nc)
        return nc, dbg
    for half in range(2):
        with nc.Block() as b:
            @b.gpsimd
            def _(g):
                for gch in range(4):
                    ins = g.dma_gather(xnT[:, gch, :, :], sc_lo[:, :],
                                       idxall[:, (half * 4 + gch) * 32:
                                              (half * 4 + gch + 1) * 32],
                                       512, 512, D, transpose=True)
                    sm.inc(ins, "gx", 16)
                sm.wait(g, "gx", sm.val("gx"))
        with nc.Block() as b:
            emit_proj(b, lo_evac, half)
    if limit <= 2:
        lower_extended_insts(nc)
        return nc, dbg
    qhi0 = PA("qhi0", [128, S], F16, 5, 9, top=True)
    qlo0 = PA("qlo0", [128, S], F16, 5, 9, top=True)
    khi0 = PA("khi0", [128, S + 128], F16, 5, 9, top=True)
    klo0 = PA("klo0", [128, S + 128], F16, 5, 9, top=True)
    qsm = PA("qsm", [4, S], F16, 5, 9, top=True)
    ksm = PA("ksm", [4, S + 128], F16, 5, 9, top=True)
    q128s = PA("q128s", [1, S], F16, 5, 9, top=True)
    k128s = PA("k128s", [1, S + 128], F16, 5, 9, top=True)
    host16 = PA("host16", [2, 2 * (S + 128)], F16, 5, 6)
    qh2 = host16[0:2, 0:S + 128]
    ql2 = host16[0:2, S + 128:2 * (S + 128)]

    with nc.Block() as b:
        @b.vector
        def _(v):
            v.tensor_copy(kph[:, S:S + 1], k0c32[:])
            for sc in range(8):
                sl = slice(sc * 512, sc * 512 + 512)
                v.tensor_copy(qhi0[:, sl], qph[:, sl])
                v.tensor_copy(khi0[:, sl], kph[:, sl])
                v.tensor_copy(qh2[:, sl], qk128h[:, sl])
                v.scalar_tensor_tensor(qlo0[:, sl], qph[:, sl], 1.0, qhi0[:, sl],
                                       op0=ALU.mult, op1=ALU.subtract)
                v.scalar_tensor_tensor(klo0[:, sl], kph[:, sl], 1.0, khi0[:, sl],
                                       op0=ALU.mult, op1=ALU.subtract)
                v.scalar_tensor_tensor(ql2[:, sl], qk128h[:, sl], 1.0, qh2[:, sl],
                                       op0=ALU.mult, op1=ALU.subtract)
            v.tensor_copy(khi0[:, S:S + 1], kph[:, S:S + 1])
            v.tensor_copy(qh2[0:2, S:S + 1], k0s[0:2, :])
            v.memset(ksm[0:1, :], 1.0)
            v.scalar_tensor_tensor(klo0[:, S:S + 1], kph[:, S:S + 1], 1.0,
                                   khi0[:, S:S + 1], op0=ALU.mult, op1=ALU.subtract)
            v.scalar_tensor_tensor(ql2[0:2, S:S + 1], k0s[0:2, :], 1.0,
                                   qh2[0:2, S:S + 1], op0=ALU.mult, op1=ALU.subtract)
            ins = v.tensor_copy(q128s[0:1, 0:S], qh2[0:1, 0:S])
            sm.inc(ins, "sp_v")
        @b.scalar
        def _(s):
            sm.wait(s, "sp_v", 1)
            for dst, src in [
                (ksm[1:2, 0:S + 1], qh2[1:2, 0:S + 1]),
                (k128s[0:1, 0:S + 1], qh2[1:2, 0:S + 1]),
                (ksm[2:3, 0:S + 1], ql2[1:2, 0:S + 1]),
                (ksm[3:4, 0:S + 1], qh2[1:2, 0:S + 1]),
                (qsm[1:2, 0:S], qh2[0:1, 0:S]),
                (qsm[2:3, 0:S], qh2[0:1, 0:S]),
                (qsm[3:4, 0:S], ql2[0:1, 0:S]),
            ]:
                ins = s.dma_start(dst, src)
                sm.inc(ins, "sp_d", 16)
            sm.wait(s, "sp_d", sm.val("sp_d"))
    if limit <= 3:
        lower_extended_insts(nc)
        return nc, dbg
    vat = PA("vat", [128, TC, D], F16, 6, 9, top=True)
    xnTh = PA("xnTh", [128, 2, DC, 512], F16, 6, 7)

    for quar in range(4):
        with nc.Block() as b:
            @b.gpsimd
            def _(g):
                for gch in range(2):
                    ins = g.dma_gather(xnTh[:, gch, :, :], sc_hi[:, 0:D],
                                       idxall[:, (quar * 2 + gch) * 32:
                                              (quar * 2 + gch + 1) * 32],
                                       512, 512, D, elem_step=ROWB, transpose=True)
                    sm.inc(ins, "gx", 16)
                sm.wait(g, "gx", sm.val("gx"))
        base_vt = sm.val("vb_t")
        base_vv = sm.val("vb_v")
        with nc.Block() as b:
            @b.tensor
            def _(t):
                for i in range(TC // 4):
                    psel = pAV if i % 2 == 0 else pT
                    if i >= 2:
                        sm.wait(t, "vb_v", base_vv + i - 1)
                    ins = None
                    for cc in range(DC):
                        for hn in range(2):
                            ins = t.matmul(psel[:, hn, :],
                                           xnTh[:, i // 4, cc, (i % 4) * 128:
                                                (i % 4) * 128 + 128],
                                           wvTt[:, cc, hn * 512:(hn + 1) * 512],
                                           start=(cc == 0), stop=(cc == DC - 1))
                    sm.inc(ins, "vb_t")
            @b.vector
            def _(v):
                for i in range(TC // 4):
                    psel = pAV if i % 2 == 0 else pT
                    sm.wait(v, "vb_t", base_vt + i + 1)
                    ti = quar * (TC // 4) + i
                    v.tensor_copy(vat[:, ti, 0:512], psel[:, 0, :])
                    ins = v.tensor_copy(vat[:, ti, 512:1024], psel[:, 1, :])
                    sm.inc(ins, "vb_v")


    if "qk" in debug:
        dbg_dump([("qhi", [128, S], F16, qhi0[:]), ("qlo", [128, S], F16, qlo0[:]),
                  ("khi", [128, S + 128], F16, khi0[:]),
                  ("klo", [128, S + 128], F16, klo0[:]),
                  ("qsm", [4, S], F16, qsm[:]), ("ksm", [4, S + 128], F16, ksm[:]),
                  ("vat", [128, TC, D], F16, vat[:]),
                  ("zc", [128, NT], F32, zcol[:])])

    if limit <= 4:
        lower_extended_insts(nc)
        return nc, dbg
    with nc.Block() as b:
        @b.tensor
        def _(t):
            ins = t.matmul(pA[0:32, 0, 0:128], zcol[:], id32(128, 128),
                           start=True, stop=True, is_transpose=True)
            sm.inc(ins, "tk_t")
        @b.vector
        def _(v):
            sm.wait(v, "tk_t", 1)
            v.tensor_copy(zTt[:], pA[0:32, 0, 0:128])
            v.tensor_reduce(junk32[0:32, 0:1], pA[0:32, 0, 0:128], axis=AX.X,
                            op=ALU.max)
            ins = v.tensor_reduce(junk32[0:32, 1:2], pA[0:32, 0, 0:128], axis=AX.X,
                                  op=ALU.min, negate=True)
            sm.inc(ins, "tk_v")
        @b.tensor
        def _(t):
            sm.wait(t, "tk_v", 1)
            ins = t.matmul(pA[0:2, 1, 0:32], junk32[0:32, 0:2], id32(32, 32),
                           start=True, stop=True, is_transpose=True)
            sm.inc(ins, "tk_t2")
        @b.vector
        def _(v):
            sm.wait(v, "tk_t2", 1)
            ins = v.tensor_reduce(cnt[0:2, 3:4], pA[0:2, 1, 0:32], axis=AX.X,
                                  op=ALU.max)
            sm.inc(ins, "tk_v2")
        @b.tensor
        def _(t):
            sm.wait(t, "tk_v2", 1)
            ins = t.matmul(pA[0:1, 2, 0:2], cnt[0:2, 3:4], id32(2, 2),
                           start=True, stop=True, is_transpose=True)
            sm.inc(ins, "tk_t3")
        @b.vector
        def _(v):
            sm.wait(v, "tk_t3", 1)
            ins = v.tensor_single_scalar(srch[0:1, 1:2], pA[0:1, 2, 0:1], 1.0,
                                         op=ALU.add)
            f1 = sm.fence(ins)
            ins = v.tensor_scalar(srch[0:1, 0:1], pA[0:1, 2, 1:2], -1.0, -1.0,
                                  op0=ALU.mult, op1=ALU.add)
            f2 = sm.fence(ins)
            sm.fwait(v, f2)
            ins = v.tensor_tensor(srch[0:1, 2:3], srch[0:1, 1:2], srch[0:1, 0:1],
                                  op=ALU.subtract)
            f3 = sm.fence(ins)
            sm.fwait(v, f3)
            ins = v.tensor_single_scalar(srch[0:1, 2:3], srch[0:1, 2:3], 1.0 / 17.0,
                                         op=ALU.mult)
            sm.inc(ins, "tk_r")

    for r in range(NROUND):
        with nc.Block() as b:
            @b.vector
            def _(v):
                if r == 0:
                    sm.wait(v, "tk_r", 1)
                else:
                    sm.wait(v, "sr_d", r)
                ins = v.tensor_scalar(mids[:], j16row, srch[0:1, 2:3], srch[0:1, 0:1],
                                      op0=ALU.mult, op1=ALU.add)
                sm.inc(ins, "sr_m")
            @b.tensor
            def _(t):
                sm.wait(t, "sr_m", sm.val("sr_m"))
                ins = t.matmul(pA[0:32, 1, 0:16], onerow(32), mids[:],
                               start=True, stop=True)
                sm.inc(ins, "sr_t")
            @b.vector
            def _(v):
                sm.wait(v, "sr_t", sm.val("sr_t"))
                ins = v.tensor_copy(midsB[:], pA[0:32, 1, 0:16])
                fmb = sm.fence(ins)
                for j in range(16):
                    if j == 2:
                        sm.fwait(v, fmb)
                    ins = v.tensor_scalar(junk32[:], zTt[:],
                                          pA[0:32, 1, j:j + 1] if j < 2
                                          else midsB[:, j:j + 1],
                                          None, op0=ALU.is_ge, op1=ALU.add,
                                          accum_out=cnt[:, j:j + 1])
                sm.inc(ins, "sr_c")
            @b.tensor
            def _(t):
                sm.wait(t, "sr_c", sm.val("sr_c"))
                ins = t.matmul(pA[0:1, 2, 0:16], onecol(32), cnt[:],
                               start=True, stop=True)
                sm.inc(ins, "sr_t2")
            @b.vector
            def _(v):
                sm.wait(v, "sr_t2", sm.val("sr_t2"))
                ins = v.tensor_scalar(midsB[0:1, 0:16], pA[0:1, 2, 0:16], 2048.0,
                                      None, op0=ALU.is_ge, op1=ALU.add,
                                      accum_out=srch[0:1, 3:4])
                fac = sm.fence(ins)
                sm.fwait(v, fac)
                ins = v.tensor_tensor(srch[0:1, 5:6], srch[0:1, 3:4], srch[0:1, 2:3],
                                      op=ALU.mult)
                f5 = sm.fence(ins)
                v.tensor_single_scalar(srch[0:1, 2:3], srch[0:1, 2:3],
                                       1.0 / 17.0, op=ALU.mult)
                sm.fwait(v, f5)
                ins = v.tensor_tensor(srch[0:1, 0:1], srch[0:1, 0:1], srch[0:1, 5:6],
                                      op=ALU.add)
                sm.inc(ins, "sr_d")

    with nc.Block() as b:
        @b.tensor
        def _(t):
            ins = t.matmul(pA[0:32, 1, 0:1], onerow(32), srch[0:1, 0:1],
                           start=True, stop=True)
            sm.inc(ins, "mk_t")
        @b.vector
        def _(v):
            sm.wait(v, "mk_t", 1)
            v.memset(junk32[:], 0.0)
            v.tensor_scalar(mhi[:], zTt[:], pA[0:32, 1, 0:1], None, op0=ALU.is_gt)
            ins = v.tensor_scalar(tie[:], zTt[:], pA[0:32, 1, 0:1], None,
                                  op0=ALU.is_equal)
            fti = sm.fence(ins)
            v.tensor_copy(loB[:, 0:1], pA[0:32, 1, 0:1])
            sm.fwait(v, fti)
            v.tensor_tensor_scan(rnkh[:], mhi[:], junk32[:], 0.0,
                                 op0=ALU.add, op1=ALU.add)
            ins = v.tensor_tensor_scan(rnkt[:], tie[:], junk32[:], 0.0,
                                       op0=ALU.add, op1=ALU.add)
            fsc = sm.fence(ins)
            v.tensor_reduce(cnt[:, 1:2], mhi[:], axis=AX.X, op=ALU.add)
            v.tensor_reduce(cnt[:, 2:3], tie[:], axis=AX.X, op=ALU.add)
            sm.fwait(v, fsc)
            v.tensor_tensor(rnkh[:], rnkh[:], mhi[:], op=ALU.subtract)
            ins = v.tensor_tensor(rnkt[:], rnkt[:], tie[:], op=ALU.subtract)
            sm.inc(ins, "mk_v")
        @b.tensor
        def _(t):
            sm.wait(t, "mk_v", 1)
            t.matmul(pA[0:32, 2, 0:2], tril32, cnt[0:32, 1:3], start=True, stop=True)
            ins = t.matmul(pA[0:1, 3, 0:1], onecol(32), cnt[0:32, 1:2],
                           start=True, stop=True)
            sm.inc(ins, "mk_t2")
        @b.vector
        def _(v):
            sm.wait(v, "mk_t2", 1)
            sm.fwait(v, sm.val("vf"))
            v.tensor_scalar(rnkh[:], rnkh[:], pA[0:32, 2, 0:1], None, op0=ALU.add)
            ins = v.tensor_scalar(rnkt[:], rnkt[:], pA[0:32, 2, 1:2], None,
                                  op0=ALU.add)
            sm.fence(ins)
            ins = v.tensor_copy(srch[0:1, 4:5], pA[0:1, 3, 0:1])
            sm.inc(ins, "mk_v2")
        @b.tensor
        def _(t):
            sm.wait(t, "mk_v2", 1)
            ins = t.matmul(pA[0:32, 1, 1:2], onerow(32), srch[0:1, 4:5],
                           start=True, stop=True)
            sm.inc(ins, "mk_t3")
        @b.vector
        def _(v):
            sm.wait(v, "mk_t3", 1)
            sm.fwait(v, sm.val("vf"))
            ins = v.tensor_copy(loB[:, 1:2], pA[0:32, 1, 1:2])
            flo = sm.fence(ins)
            ins = v.tensor_scalar(rnkh[:], rnkh[:], 2048.0, None, op0=ALU.is_lt)
            frh = sm.fence(ins)
            sm.fwait(v, flo)
            ins = v.tensor_scalar(rnkt[:], rnkt[:], loB[:, 1:2], 2048.0,
                                  op0=ALU.add, op1=ALU.is_lt)
            frt = sm.fence(ins)
            sm.fwait(v, frh)
            v.tensor_tensor(mhi[:], mhi[:], rnkh[:], op=ALU.mult)
            sm.fwait(v, frt)
            ins = v.tensor_tensor(tie[:], tie[:], rnkt[:], op=ALU.mult)
            fti = sm.fence(ins)
            ins = v.tensor_scalar(junk32[:], iotaT, 1.0, None, op0=ALU.add)
            sm.fence(ins)
            sm.fwait(v, fti)
            ins = v.tensor_tensor(mhi[:], mhi[:], tie[:], op=ALU.add)
            fm2 = sm.fence(ins)
            sm.fwait(v, fm2)
            ins = v.tensor_tensor(arr[:], mhi[:], junk32[:], op=ALU.mult)
            fa = sm.fence(ins)
            sm.fwait(v, fa)
            ins = v.tensor_single_scalar(arr[:], arr[:], -1.0, op=ALU.add)
            fa2 = sm.fence(ins)
            sm.fwait(v, fa2)
            ins = v.tensor_copy(arr16[:, 0:128], arr[0:16, :])
            sm.inc(ins, "mk_v3")
        @b.scalar
        def _(s):
            sm.wait(s, "mk_v3", 1)
            ins = s.dma_start(arr16[:, 128:256], arr[16:32, :])
            sm.inc(ins, "mk_d", 16)
            sm.wait(s, "mk_d", 16)

    with nc.Block() as b:
        @b.gpsimd
        def _(g):
            g.load_library(LC.sparse_gather)
            g.sparse_gather(sgout[:], arr16[:], num_found=nfound[:])
    with nc.Block() as b:
        @b.vector
        def _(v):
            ins = v.tensor_copy(idx16[0:16, :], sgout[:])
            sm.inc(ins, "ix_v")
        @b.scalar
        def _(s):
            sm.wait(s, "ix_v", 1)
            ins = s.dma_start(idx16[16:32, :], idx16[0:16, :])
            sm.inc(ins, "ix_d", 16)
            sm.wait(s, "ix_d", 16)
    with nc.Block() as b:
        @b.gpsimd
        def _(g):
            g.load_library(LC.mlp)

    if "idx" in debug:
        dbg_dump([("arr", [32, 128], F32, arr[:]),
                  ("sgo", [16, 128], F32, sgout[:]),
                  ("nf", [1, 1], U32, nfound[:]),
                  ("zTo", [32, 128], F32, zTt[:]),
                  ("srch", [1, 16], F32, srch[:])])

    if limit <= 5:
        lower_extended_insts(nc)
        return nc, dbg
    attnT = PA("attnT", [128, TC, 512], F16, 8, 9)
    snkT = PA("snkT", [1, 512], F16, 8, 9)
    xre = PA("xre", [128, 2, D], F32, 8, 9)
    obuf = PA("obuf", [128, 2, D], F32, 8, 9)
    tmpD2 = PA("tmpD2", [128, D], F32, 8, 9)
    nsh = PA("nsh", [128, 8], F32, 8, 9)
    padump = PA("padump", [128, 4 * 512 + 8], F32, 8, 9) if "attn" in debug else None

    with nc.Block() as b:
        @b.sync
        def _(s):
            for ti in range(NT):
                if ti >= 2:
                    sm.wait(s, "aXc", ti - 1)
                ins = s.dma_start(xre[:, ti % 2, :], x[ti * 128:(ti + 1) * 128, :])
                sm.inc(ins, "aX", 16)
        @b.gpsimd
        def _(g):
            for ti in range(NT):
                sm.wait(g, "aOb", ti + 1)
                ins = g.dma_start(y[ti * 128:(ti + 1) * 128, :], obuf[:, ti % 2, :])
                sm.inc(ins, "aYw", 16)
            sm.wait(g, "aYw", NT * 16)

        @b.tensor
        def _(t):
            for bi in range(NB):
                s0 = bi * 512
                ssl = slice(s0, s0 + 512)
                for sub in range(4):
                    ti = bi * 4 + sub
                    tsl = slice(ti * 128, ti * 128 + 128)
                    for grp in range(2):
                        gidx = ti * 2 + grp
                        if gidx >= 1:
                            sm.wait(t, "aSc", gidx)
                        if grp == 0 and ti >= 1:
                            sm.wait(t, "aNsE", ti)
                        if bi >= 1 and sub == 0 and grp == 0:
                            sm.wait(t, "aAVe", bi * 4)
                        for kc in range(grp * 4, grp * 4 + 4):
                            t.matmul(pA[:, kc % 4, :], qhi0[:, tsl],
                                     khi0[:, kc * 512:(kc + 1) * 512],
                                     start=True, stop=False)
                        for kc in range(grp * 4, grp * 4 + 4):
                            m = t.matmul(pA[:, kc % 4, :], q128s[0:1, tsl],
                                         k128s[0:1, kc * 512:(kc + 1) * 512],
                                         start=False, stop=True)
                        sm.inc(m, "aA")
                    if ti >= 1:
                        sm.wait(t, "aNs", ti)
                    if bi >= 1 and sub == 0:
                        sm.wait(t, "aAVe", bi * 4)
                    t.matmul(pAV[:, 0, 0:1], qhi0[:, tsl], khi0[:, S:S + 1],
                             start=True, stop=False)
                    m = t.matmul(pAV[:, 0, 0:1], q128s[0:1, tsl], k128s[0:1, S:S + 1],
                                 start=False, stop=True)
                    sm.inc(m, "aSnk")
                    sm.wait(t, "aNs", ti + 1)
                    sm.wait(t, "aSc", ti * 2 + 2)
                    m = t.matmul(pA[0:1, 1, 0:128], nsh[:, sub:sub + 1],
                                 id32(128, 128),
                                 start=True, stop=True, is_transpose=True)
                    sm.inc(m, "aNsT")
                sm.wait(t, "aNsE", (bi + 1) * 4)
                for tc in range(TC + 1):
                    par = tc % 2
                    gi = bi * (TC + 1) + tc
                    if gi >= 2:
                        sm.wait(t, "aE", gi - 1)
                    if tc < TC:
                        ksl = slice(tc * 128, tc * 128 + 128)
                        t.matmul(pT[:, par, :], khi0[:, ksl], qhi0[:, ssl],
                                 start=True, stop=False)
                        t.matmul(pT[:, par, :], khi0[:, ksl], qlo0[:, ssl],
                                 start=False, stop=False)
                        t.matmul(pT[:, par, :], klo0[:, ksl], qhi0[:, ssl],
                                 start=False, stop=False)
                        m = t.matmul(pT[:, par, :], ksm[:, ksl], qsm[:, ssl],
                                     start=False, stop=True)
                    else:
                        t.matmul(pT[0:1, par, :], khi0[:, S:S + 1], qhi0[:, ssl],
                                 start=True, stop=False)
                        t.matmul(pT[0:1, par, :], khi0[:, S:S + 1], qlo0[:, ssl],
                                 start=False, stop=False)
                        t.matmul(pT[0:1, par, :], klo0[:, S:S + 1], qhi0[:, ssl],
                                 start=False, stop=False)
                        m = t.matmul(pT[0:1, par, :], ksm[:, S:S + 1], qsm[:, ssl],
                                     start=False, stop=True)
                    sm.inc(m, "aT")
                for sub in range(4):
                    ti = bi * 4 + sub
                    q0 = sub * 128
                    if ti >= 1:
                        sm.wait(t, "aAVe", ti)
                    if sub == 0:
                        sm.wait(t, "aE", bi * (TC + 1) + TC + 1)
                    for tc in range(TC):
                        lh = attnT[:, tc, q0:q0 + 128]
                        t.matmul(pAV[:, 0, :], lh, vat[:, tc, 0:512],
                                 start=(tc == 0), stop=False)
                        t.matmul(pAV[:, 1, :], lh, vat[:, tc, 512:1024],
                                 start=(tc == 0), stop=False)
                        t.matmul(pA[:, 0, 0:1], lh, ones16[:],
                                 start=(tc == 0), stop=False)
                    t.matmul(pAV[:, 0, :], snkT[0:1, q0:q0 + 128], vsink[0:1, 0:512],
                             start=False, stop=True)
                    t.matmul(pAV[:, 1, :], snkT[0:1, q0:q0 + 128],
                             vsink[0:1, 512:1024], start=False, stop=True)
                    m = t.matmul(pA[:, 0, 0:1], snkT[0:1, q0:q0 + 128],
                                 ones16[0:1, :], start=False, stop=True)
                    sm.inc(m, "aAV")

        @b.vector
        def _(v):
            for bi in range(NB):
                for sub in range(4):
                    ti = bi * 4 + sub
                    sm.wait(v, "aA", ti * 2 + 1)
                    ins = v.tensor_reduce(nsh[:, 4:5],
                                          pA[:, 0:4, :].rearrange("p a b -> p (a b)"),
                                          axis=AX.X, op=ALU.max)
                    sm.inc(ins, "aSc")
                    sm.wait(v, "aA", ti * 2 + 2)
                    ins = v.tensor_reduce(nsh[:, 5:6],
                                          pA[:, 0:4, :].rearrange("p a b -> p (a b)"),
                                          axis=AX.X, op=ALU.max)
                    frd = sm.inc(ins, "aSc")
                    if padump is not None and ti == 0:
                        v.tensor_copy(padump[:, 0:2048],
                                      pA[:, 0:4, :].rearrange("p a b -> p (a b)"))
                        v.tensor_copy(padump[:, 2048:2049], nsh[:, 4:5])
                        v.tensor_copy(padump[:, 2049:2050], nsh[:, 5:6])
                        v.tensor_copy(padump[:, 2050:2051], pAV[:, 0, 0:1])
                    sm.wait(v, "aSnk", ti + 1)
                    sm.wait(v, "aSc", frd)
                    ins = v.tensor_scalar(nsh[:, 6:7], nsh[:, 4:5], nsh[:, 5:6],
                                          pAV[:, 0, 0:1], op0=ALU.max, op1=ALU.max)
                    f6 = sm.fence(ins)
                    sm.fwait(v, f6)
                    ins = v.tensor_scalar(nsh[:, sub:sub + 1], nsh[:, 6:7],
                                          MARGIN, -1.0, op0=ALU.add, op1=ALU.mult)
                    sm.inc(ins, "aNs")
                    sm.wait(v, "aNsT", ti + 1)
                    ins = v.tensor_copy(qsm[0:1, ti * 128:ti * 128 + 128],
                                        pA[0:1, 1, 0:128])
                    sm.inc(ins, "aNsE")
                for sub in range(4):
                    ti = bi * 4 + sub
                    sm.wait(v, "aAV", ti + 1)
                    ins = v.reciprocal(nsh[:, 5:6], pA[:, 0, 0:1])
                    frc = sm.fence(ins)
                    v.tensor_single_scalar(nsh[:, 6:7], rstdcol[:, ti:ti + 1],
                                           1.0, op=ALU.add)
                    ins = v.tensor_tensor(nsh[:, 7:8], mcol[:, ti:ti + 1],
                                          rstdcol[:, ti:ti + 1], op=ALU.mult)
                    sm.fence(ins)
                    sm.fwait(v, frc)
                    v.scalar_tensor_tensor(obuf[:, ti % 2, 0:512], pAV[:, 0, :],
                                           nsh[:, 5:6], bvbc[:, 0:512],
                                           op0=ALU.mult, op1=ALU.add)
                    ins = v.scalar_tensor_tensor(obuf[:, ti % 2, 512:1024],
                                                 pAV[:, 1, :], nsh[:, 5:6],
                                                 bvbc[:, 512:1024],
                                                 op0=ALU.mult, op1=ALU.add)
                    sm.inc(ins, "aAVe")
                    sm.wait(v, "aX", (ti + 1) * 16)
                    ins = v.tensor_scalar(tmpD2[:], xre[:, ti % 2, :], nsh[:, 6:7],
                                          nsh[:, 7:8],
                                          op0=ALU.mult, op1=ALU.subtract)
                    sm.inc(ins, "aXc")
                    if ti >= 2:
                        sm.wait(v, "aYw", (ti - 1) * 16)
                    sm.wait(v, "aXc", ti + 1)
                    ins = v.tensor_tensor(obuf[:, ti % 2, :], obuf[:, ti % 2, :],
                                          tmpD2[:], op=ALU.add)
                    sm.inc(ins, "aOb")

        @b.scalar
        def _(s):
            for bi in range(NB):
                if bi >= 1:
                    sm.wait(s, "aAV", bi * 4)
                for tc in range(TC + 1):
                    sm.wait(s, "aT", bi * (TC + 1) + tc + 1)
                    par = tc % 2
                    if tc < TC:
                        ins = s.activation(attnT[:, tc, :], pT[:, par, :], AF.Exp)
                    else:
                        ins = s.activation(snkT[:], pT[0:1, par, :], AF.Exp)
                    sm.inc(ins, "aE")



    if "attn" in debug:
        dbg_dump([("pad", [128, 4 * 512 + 8], F32, padump[:]),
                  ("qsm0", [1, S], F16, qsm[0:1, :]),
                  ("attnT", [128, TC, 512], F16, attnT[:]),
                  ("snkT", [1, 512], F16, snkT[:])])

    if limit <= 6:
        lower_extended_insts(nc)
        return nc, dbg
    fqT = PA("fqT", [128, DC, C], F16, 9, 11, top=True)
    fkT = PA("fkT", [128, DC, C], F16, 9, 11, top=True)
    fv = PA("fv", [128, FTC, 1040], F16, 9, 11, top=True)
    Wb0 = PA("Wb0", [128, DC, D], F16, 9, 10)
    Wb1 = Wb0
    ainT = PA("ainT", [128, 4, 9, 512], F16, 9, 10)
    zsel = PA("zsel", [1, C], F16, 9, 10)
    gs16 = PA("gs16", [16, 128], F32, 9, 10)

    with nc.Block() as b:
        @b.gpsimd
        def _(g):
            for gch in range(4):
                ins = g.dma_gather(ainT[:, gch, :, :], sc_hi[:, :],
                                   idx16[:, gch * 32:(gch + 1) * 32],
                                   512, 512, ROWB, transpose=True)
                sm.inc(ins, "fg", 16)
            ins = g.dma_start(Wb0[:], Wqf[:, :].rearrange("(c p) n -> p c n", p=128))
            sm.inc(ins, "fW", 16)
            sm.wait(g, "fg", sm.val("fg"))
            sm.wait(g, "fW", sm.val("fW"))

    with nc.Block() as b:
        @b.scalar
        def _(s):
            for gch in range(4):
                ins = s.activation(zsel[0:1, gch * 512:(gch + 1) * 512],
                                   ainT[0:1, gch, 8, 0:512], AF.Sigmoid)
            sm.inc(ins, "gz_s")
        @b.gpsimd
        def _(g):
            sm.wait(g, "gz_s", sm.val("gz_s"))
            ins = g.dma_start(sc_z[0:1, :, :],
                              zsel[0:1, :].rearrange("o (a b) -> o a b", b=128))
            sm.inc(ins, "gz_d", 16)
            sm.wait(g, "gz_d", sm.val("gz_d"))
            ins = g.dma_start(gs16[:], sc_z[0, :, :])
            sm.inc(ins, "gz_d", 16)
        @b.tensor
        def _(t):
            sm.wait(t, "gz_d", sm.val("gz_d"))
            ins = t.matmul(pA[:, 0, 0:16], gs16[:], id32(16, 16),
                           start=True, stop=True, is_transpose=True)
            sm.inc(ins, "gz_t")
        @b.vector
        def _(v):
            sm.wait(v, "gz_t", sm.val("gz_t"))
            v.tensor_copy(gsel[:], pA[:, 0, 0:16])

    if limit <= 6.2:
        lower_extended_insts(nc)
        return nc, dbg

    def ffn_proj(b, wbuf, wready, nxt, evac):
        base_t = sm.val("fp_t")
        base_v = sm.val("fp_v")
        @b.tensor
        def _(t):
            sm.wait(t, "fW", wready)
            for jc in range(DC):
                if jc >= 2:
                    sm.wait(t, "fp_v", base_v + jc - 1)
                for cc in range(DC):
                    for nb_ in range(4):
                        ps = (pA[:, nb_, :] if jc % 2 == 0
                              else (pT[:, nb_, :] if nb_ < 2 else pAV[:, nb_ - 2, :]))
                        ins = t.matmul(ps, wbuf[:, cc, jc * 128:(jc + 1) * 128],
                                       ainT[:, nb_, cc, 0:512],
                                       start=(cc == 0), stop=(cc == DC - 1))
                sm.inc(ins, "fp_t")
        @b.vector
        def _(v):
            for jc in range(DC):
                sm.wait(v, "fp_t", base_t + jc + 1)
                for nb_ in range(4):
                    ps = (pA[:, nb_, :] if jc % 2 == 0
                          else (pT[:, nb_, :] if nb_ < 2 else pAV[:, nb_ - 2, :]))
                    ins = evac(v, jc, nb_, ps)
                sm.inc(ins, "fp_v")

    with nc.Block() as b:
        def ev_q(v, jc, nb_, ps):
            return v.tensor_scalar(fqT[:, jc, nb_ * 512:(nb_ + 1) * 512], ps,
                                   0.125, None, op0=ALU.mult)
        ffn_proj(b, Wb0, 16, None, ev_q)
    if limit <= 6.3:
        lower_extended_insts(nc)
        return nc, dbg
    with nc.Block() as b:
        @b.gpsimd
        def _(g):
            ins = g.dma_start(Wb0[:], Wkf[:, :].rearrange("(c p) n -> p c n", p=128))
            sm.inc(ins, "fW", 16)
            sm.wait(g, "fW", sm.val("fW"))
    with nc.Block() as b:
        def ev_k(v, jc, nb_, ps):
            return v.tensor_copy(fkT[:, jc, nb_ * 512:(nb_ + 1) * 512], ps)
        ffn_proj(b, Wb1, 32, None, ev_k)
    with nc.Block() as b:
        @b.gpsimd
        def _(g):
            ins = g.dma_start(Wb0[:], Wvf[:, :].rearrange("(c p) n -> p c n", p=128))
            sm.inc(ins, "fW", 16)
            sm.wait(g, "fW", sm.val("fW"))

    with nc.Block() as b:
        @b.vector
        def _(v):
            ins = None
            for ct in range(FTC):
                for h in range(H):
                    ins = v.memset(fv[:, ct, h * 65 + 64:h * 65 + 65], 1.0)
            sm.inc(ins, "fv0")
        @b.tensor
        def _(t):
            sm.wait(t, "fv0", 1)
            sm.wait(t, "fW", 48)
            for ct in range(FTC):
                par = ct % 2
                if ct >= 2:
                    sm.wait(t, "fv_v", ct - 1)
                ins = None
                for cc in range(DC):
                    for hn in range(2):
                        ps = pA[:, 2 * par + hn, :]
                        ins = t.matmul(ps, ainT[:, ct // 4, cc,
                                               (ct % 4) * 128:(ct % 4) * 128 + 128],
                                       Wb0[:, cc, hn * 512:(hn + 1) * 512],
                                       start=(cc == 0), stop=(cc == DC - 1))
                sm.inc(ins, "fv_t")
        @b.vector
        def _(v):
            for ct in range(FTC):
                par = ct % 2
                sm.wait(v, "fv_t", ct + 1)
                dst = fv[:, ct, :].rearrange("p (h i) -> p h i", i=65)
                v.tensor_copy(dst[:, 0:8, 0:64],
                              pA[:, 2 * par, :].rearrange("p (h i) -> p h i", i=64))
                ins = v.tensor_copy(dst[:, 8:16, 0:64],
                                    pA[:, 2 * par + 1, :].rearrange("p (h i) -> p h i", i=64))
                sm.inc(ins, "fv_v")

    if limit <= 7:
        lower_extended_insts(nc)
        return nc, dbg
    khat = [PA(f"khat{i}", [65, C], F16, 10, 11) for i in range(2)]
    qhat = [PA(f"qhat{i}", [65, C], F16, 10, 11) for i in range(2)]
    sqh = PA("sqh", [128, C], F16, 10, 11)
    qn2 = PA("qn2", [2, C], F16, 10, 11)
    kn2 = PA("kn2", [2, C], F16, 10, 11)
    qns = qn2
    negB = PA("negB", [2, C], F16, 10, 11)
    kmx = PA("kmx", [2, 2], F32, 10, 11)
    fattnT = PA("fattnT", [128, 2, FTC, 512], F16, 10, 11)
    o16 = PA("o16", [128, FTC, D], F16, 10, 12, top=True)
    frec = PA("frec", [128, 2], F32, 10, 11)

    def head_q(h):
        return fqT[(h % 2) * 64:(h % 2) * 64 + 64, h // 2, :]
    def head_k(h):
        return fkT[(h % 2) * 64:(h % 2) * 64 + 64, h // 2, :]

    for jc in range(DC if limit > 7.3 else 1):
        with nc.Block() as b:
            @b.vector
            def _(v):
                if jc == 0:
                    v.memset(khat[0][64:65, :], 1.0)
                    v.memset(khat[1][64:65, :], 1.0)
                ins = v.tensor_tensor(sqh[:], fqT[:, jc, :], fqT[:, jc, :],
                                      op=ALU.mult)
                sm.inc(ins, "fb_v")
            @b.tensor
            def _(t):
                sm.wait(t, "fb_v", sm.val("fb_v"))
                for nb_ in range(4):
                    ins = t.matmul(pA[0:2, nb_, :], blk2[:, :],
                                   sqh[:, nb_ * 512:(nb_ + 1) * 512],
                                   start=True, stop=True)
                sm.inc(ins, "fb_t")
            if limit > 7.31:
                @b.vector
                def _(v):
                    sm.wait(v, "fb_t", sm.val("fb_t"))
                    for nb_ in range(4):
                        v.tensor_copy(qn2[0:2, nb_ * 512:(nb_ + 1) * 512],
                                      pA[0:2, nb_, :])
                    ins = v.tensor_tensor(sqh[:], fkT[:, jc, :], fkT[:, jc, :],
                                          op=ALU.mult)
                    sm.inc(ins, "fb_v2")
                @b.tensor
                def _(t):
                    sm.wait(t, "fb_v2", sm.val("fb_v2"))
                    for nb_ in range(4):
                        ins = t.matmul(pA[0:2, nb_, :], blk2[:, :],
                                       sqh[:, nb_ * 512:(nb_ + 1) * 512],
                                       start=True, stop=True)
                    sm.inc(ins, "fb_t2")
            if limit > 7.32:
                @b.vector
                def _(v):
                    sm.wait(v, "fb_t2", sm.val("fb_t2"))
                    for nb_ in range(4):
                        ins = v.tensor_copy(kn2[0:2, nb_ * 512:(nb_ + 1) * 512],
                                            pA[0:2, nb_, :])
                    fkn = sm.fence(ins)
                    sm.fwait(v, fkn)
                    ins = v.tensor_reduce(kmx[0:2, 0:1], kn2[0:2, :], axis=AX.X,
                                          op=ALU.max)
                    sm.inc(ins, "fb_v3")
            if limit > 7.33:
                @b.scalar
                def _(s):
                    sm.wait(s, "fb_v3", sm.val("fb_v3"))
                    s.activation(qns[0:2, :], qn2[0:2, :], AF.Sqrt)
                    ins = s.activation(kmx[0:2, 1:2], kmx[0:2, 0:1], AF.Sqrt)
                    sm.inc(ins, "fb_s")
                @b.vector
                def _(v):
                    sm.wait(v, "fb_s", sm.val("fb_s"))
                    ins = v.tensor_scalar(qns[0:2, :], qns[0:2, :], kmx[0:2, 1:2],
                                          MARGIN, op0=ALU.mult, op1=ALU.add)
                    fq = sm.fence(ins)
                    sm.fwait(v, fq)
                    ins = v.tensor_scalar(negB[0:2, :], qns[0:2, :], -1.0, None,
                                          op0=ALU.mult)
                    sm.fence(ins)
        if limit <= 7.5:
            continue
        base_d = sm.val("fh_d")
        base_t = sm.val("fh_t")
        base_e = sm.val("fh_e")
        base_av = sm.val("fh_av")
        base_ave = sm.val("fh_ave")
        if limit <= 7.7 and jc >= 1:
            continue
        with nc.Block() as b:
            @b.scalar
            def _(s):
                for hh in range(2):
                    s.dma_start(khat[hh][0:64, :], head_k(jc * 2 + hh)).then_inc(
                        sm.sem("fh_d"), 16)
                    sm.v["fh_d"] += 16
                    s.dma_start(qhat[hh][0:64, :], head_q(jc * 2 + hh)).then_inc(
                        sm.sem("fh_d"), 16)
                    sm.v["fh_d"] += 16
                    ins = s.dma_start(qhat[hh][64:65, :], negB[hh:hh + 1, :])
                    sm.inc(ins, "fh_d", 16)
                for hh in range(2):
                    for qb in range(FQB):
                        lq = hh * FQB + qb
                        if lq >= 2:
                            sm.wait(s, "fh_av", base_av + (lq - 1) * 4)
                        for pj in range(8):
                            gp = hh * 32 + qb * 8 + pj
                            sm.wait(s, "fh_t", base_t + gp + 1)
                            ins = s.activation(
                                fattnT[:, qb % 2, 2 * pj:2 * pj + 2, :],
                                pA[:, (2 * pj) % 4:(2 * pj) % 4 + 2, :], AF.Exp)
                            sm.inc(ins, "fh_e")
            @b.tensor
            def _(t):
                for hh in range(2):
                    def emit_sT(qb):
                        for tc in range(FTC):
                            gi = hh * 64 + qb * FTC + tc
                            if gi >= 4:
                                sm.wait(t, "fh_e", base_e + (gi - 4) // 2 + 1)
                            if qb == 0 and tc == 0:
                                sm.wait(t, "fh_d", base_d + (hh + 1) * 48)
                            m = t.matmul(pA[:, tc % 4, :],
                                         khat[hh][0:65, tc * 128:(tc + 1) * 128],
                                         qhat[hh][0:65, qb * 512:(qb + 1) * 512],
                                         start=True, stop=True)
                            if tc % 2 == 1:
                                sm.inc(m, "fh_t")

                    def emit_AV(qb):
                        for sub in range(4):
                            gav = hh * 16 + qb * 4 + sub
                            if gav >= 1:
                                sm.wait(t, "fh_ave", base_ave + gav - 1)
                            sm.wait(t, "fh_e", base_e + hh * 32 + (qb + 1) * 8)
                            for tc in range(FTC):
                                m = t.matmul(pAV[:, gav % 2, 0:65],
                                             fattnT[:, qb % 2, tc,
                                                    sub * 128:sub * 128 + 128],
                                             fv[:, tc, (jc * 2 + hh) * 65:
                                                (jc * 2 + hh) * 65 + 65],
                                             start=(tc == 0), stop=(tc == FTC - 1))
                            sm.inc(m, "fh_av")

                    emit_sT(0)
                    emit_sT(1)
                    emit_AV(0)
                    emit_sT(2)
                    emit_AV(1)
                    emit_sT(3)
                    emit_AV(2)
                    emit_AV(3)
            @b.vector
            def _(v):
                for hh in range(2):
                    h = jc * 2 + hh
                    for qb in range(FQB):
                        for sub in range(4):
                            gav = hh * 16 + qb * 4 + sub
                            ct = qb * 4 + sub
                            sm.wait(v, "fh_av", base_av + gav + 1)
                            ins = v.reciprocal(frec[:, 0:1], pAV[:, gav % 2, 64:65])
                            fre = sm.fence(ins)
                            sm.fwait(v, fre)
                            ins = v.tensor_scalar(o16[:, ct, h * 64:(h + 1) * 64],
                                                  pAV[:, gav % 2, 0:64],
                                                  frec[:, 0:1], None, op0=ALU.mult)
                            sm.inc(ins, "fh_ave")

    oT = PA("oT", [128, DC, C], F16, 11, 12)
    Wo16 = PA("Wo16", [128, DC, D], F16, 11, 12)
    scat = PA("scat", [128, FTC, D], F32, 11, 12)

    with nc.Block() as b:
        @b.gpsimd
        def _(g):
            ins = g.dma_start(Wo16[:], Wof[:, :].rearrange("(c p) n -> p c n", p=128))
            sm.inc(ins, "fW", 16)
            sm.wait(g, "fW", sm.val("fW"))
        @b.tensor
        def _(t):
            for k in range(FTC * DC):
                ct, ec = divmod(k, DC)
                if k >= 2:
                    sm.wait(t, "ot_v", k - 1)
                ins = t.matmul(pA16[:, k % 2, 0:128],
                               o16[:, ct, ec * 128:(ec + 1) * 128], id16[:],
                               start=True, stop=True, is_transpose=True)
                sm.inc(ins, "ot_t")
        @b.vector
        def _(v):
            for k in range(FTC * DC):
                ct, ec = divmod(k, DC)
                sm.wait(v, "ot_t", k + 1)
                ins = v.tensor_copy(oT[:, ec, ct * 128:(ct + 1) * 128],
                                    pA16[:, k % 2, 0:128])
                sm.inc(ins, "ot_v")

    with nc.Block() as b:
        @b.tensor
        def _(t):
            for ct in range(FTC):
                par = ct % 2
                if ct >= 2:
                    sm.wait(t, "wo_v", ct - 1)
                ins = None
                for jn in range(2):
                    ps = pAV[:, jn, :] if par == 0 else pT[:, jn, :]
                    for ec in range(DC):
                        ins = t.matmul(ps, oT[:, ec, ct * 128:(ct + 1) * 128],
                                       Wo16[:, ec, jn * 512:(jn + 1) * 512],
                                       start=(ec == 0), stop=(ec == DC - 1))
                sm.inc(ins, "wo_t")
        @b.vector
        def _(v):
            for ct in range(FTC):
                par = ct % 2
                sm.wait(v, "wo_t", ct + 1)
                for jn in range(2):
                    ps = pAV[:, jn, :] if par == 0 else pT[:, jn, :]
                    ins = v.tensor_scalar(scat[:, ct, jn * 512:(jn + 1) * 512], ps,
                                          gsel[:, ct:ct + 1], None, op0=ALU.mult)
                sm.inc(ins, "wo_v")

    if "ffn" in debug:
        dbg_dump([("o16", [128, FTC, D], F16, o16[:]),
                  ("scat", [128, FTC, D], F32, scat[:]),
                  ("gsel", [128, 16], F32, gsel[:])])

    with nc.Block() as b:
        @b.gpsimd
        def _(g):
            ins = g.dma_scatter_add(y[:, :], scat[:], idx16[:], C, C, D)
            sm.inc(ins, "fin", 16)
            sm.wait(g, "fin", 16)

    lower_extended_insts(nc)
    return nc, dbg


CONSTS = None
_PROG = None


def kernel(**inputs):
    global CONSTS, _PROG
    if CONSTS is None:
        CONSTS = build_consts()
    c32v, c16v, iav = CONSTS
    if _PROG is None:
        _PROG = build_program()
    nc, _ = _PROG
    B = 8
    xin = np.ascontiguousarray(inputs["L0_attn_x"], dtype=np.float32)
    base = {
        "wq": np.ascontiguousarray(inputs["L0_attn_w_q"], np.float32),
        "bq": np.ascontiguousarray(inputs["L0_attn_b_q"], np.float32).reshape(129, 1),
        "wk": np.ascontiguousarray(inputs["L0_attn_w_k"], np.float32),
        "bk": np.ascontiguousarray(inputs["L0_attn_b_k"], np.float32).reshape(129, 1),
        "wv": np.ascontiguousarray(inputs["L0_attn_w_v"], np.float32),
        "bv": np.ascontiguousarray(inputs["L0_attn_b_v"], np.float32).reshape(1, D),
        "k0": np.ascontiguousarray(inputs["L0_attn_k_0"], np.float32).reshape(129, 1),
        "v0": np.ascontiguousarray(inputs["L0_attn_v_0"], np.float32).reshape(1, D),
        "wg": np.ascontiguousarray(inputs["L0_ffn_Wg"], np.float32).reshape(1, D),
        "Wqf": np.ascontiguousarray(inputs["L0_ffn_Wq"], np.float32),
        "Wkf": np.ascontiguousarray(inputs["L0_ffn_Wk"], np.float32),
        "Wvf": np.ascontiguousarray(inputs["L0_ffn_Wv"], np.float32),
        "Wof": np.ascontiguousarray(inputs["L0_ffn_Wo"], np.float32),
        "cst32": c32v, "cst16": c16v, "csti": iav,
    }
    in_maps = [dict(base, x=np.ascontiguousarray(xin[bb])) for bb in range(B)]
    res = run_bass_kernel_spmd(nc, in_maps, core_ids=list(range(B)))
    out = np.stack([res.results[bb]["y"] for bb in range(B)], axis=0)
    return out
